# version 18
# baseline (speedup 1.0000x reference)
import contextlib
import numpy as np

import concourse.bass as bass
import concourse.mybir as mybir
from concourse.alu_op_type import AluOpType as ALU
from concourse.bass_utils import run_bass_kernel_spmd

F32 = mybir.dt.float32
BF16 = mybir.dt.bfloat16
AF = mybir.ActivationFunctionType
AX = mybir.AxisListType

NCORES = 8
B_TOT, S_TOT, D, F = 32, 2048, 1024, 4096
B = B_TOT // NCORES
TS = 64
ROWS = B * TS
NT_C = ROWS // 128
GC = 2
GROWS = ROWS * GC
NT_G = GROWS // 128
NCH = S_TOT // TS
NG = NCH // GC
DC = D // 128
FC = F // 128
BN = DC * B
NEWTON_ITERS = 7
PI = float(np.pi)
PARANOID = False


class Sched:

    def __init__(self, nc, es):
        self.nc = nc
        self.es = es
        self.events = {"tensor": [], "vector": [], "scalar": [], "sync": [],
                       "gpsimd": []}
        self.sems = {}
        self.counts = {}

    def sem(self, name):
        if name not in self.sems:
            self.sems[name] = self.es.enter_context(self.nc.semaphore(name))
            self.counts[name] = 0
        return self.sems[name]

    def c(self, name):
        self.sem(name)
        return self.counts[name]

    def op(self, eng, fn, incs=(), waits=()):
        wl = []
        for w in waits:
            if isinstance(w, str):
                self.sem(w)
                wl.append((w, self.counts[w]))
            else:
                self.sem(w[0])
                wl.append((w[0], w[1]))
        il = []
        for s, k in incs:
            self.sem(s)
            self.counts[s] += k
            il.append((s, k))
        self.events[eng].append((fn, wl, il))

    def wait(self, eng, *sem_names):
        self.op(eng, None, waits=list(sem_names))

    def replay(self, block):
        def mk(eng_name):
            def body(eng):
                for fn, wl, il in self.events[eng_name]:
                    for s, v in wl:
                        if v > 0:
                            eng.wait_ge(self.sems[s], v)
                    if fn is None:
                        continue
                    if fn == "DRAIN":
                        eng.drain()
                        continue
                    inst = fn(eng)
                    for s, k in il:
                        inst.then_inc(self.sems[s], k)
            return body
        block.tensor(mk("tensor"))
        block.vector(mk("vector"))
        block.scalar(mk("scalar"))
        block.sync(mk("sync"))
        block.gpsimd(mk("gpsimd"))


def build_nc():
    nc = bass.Bass("TRN2", target_bir_lowering=False,
                   detect_race_conditions=False)
    es = contextlib.ExitStack()

    x_h = nc.dram_tensor("x", [B, S_TOT, D], F32, kind="ExternalInput").ap()
    bwt_h = nc.dram_tensor("bwt", [D, D], F32, kind="ExternalInput").ap()
    wot_h = nc.dram_tensor("wot", [D, D], F32, kind="ExternalInput").ap()
    w1_h = nc.dram_tensor("w1", [D, F], F32, kind="ExternalInput").ap()
    w2_h = nc.dram_tensor("w2", [F, D], F32, kind="ExternalInput").ap()
    a_h = nc.dram_tensor("a_dec", [128, D // 128], F32, kind="ExternalInput").ap()
    bo_h = nc.dram_tensor("bo", [D], F32, kind="ExternalInput").ap()
    b1_h = nc.dram_tensor("b1", [128, F // 128], F32, kind="ExternalInput").ap()
    th_h = nc.dram_tensor("theta", [128, F // 128], F32, kind="ExternalInput").ap()
    wr_h = nc.dram_tensor("wre", [128, F // 128], F32, kind="ExternalInput").ap()
    wi_h = nc.dram_tensor("wim", [128, F // 128], F32, kind="ExternalInput").ap()
    b2_h = nc.dram_tensor("b2", [D], F32, kind="ExternalInput").ap()
    o_h = nc.dram_tensor("o", [B, S_TOT, D], F32, kind="ExternalOutput").ap()
    w1scr = nc.dram_tensor("w1scr", [8, 128, 4 * DC * 128], BF16,
                           kind="Internal").ap()
    w2scr = nc.dram_tensor("w2scr", [8, 128, 4 * D], BF16,
                           kind="Internal").ap()

    def sbt(name, shape, dtype, **kw):
        return es.enter_context(nc.sbuf_tensor(name, shape, dtype, **kw))

    with es:
        xy = [sbt(f"xy{i}", [128, NT_G, D], F32) for i in range(2)]
        xln = sbt("xln", [128, NT_C, D], BF16)
        xlnf = sbt("xlnf", [128, DC, ROWS], BF16)
        bxb = sbt("bxb", [128, TS * BN], F32)
        hbuf = sbt("hbuf", [128, TS * BN], F32)
        hbf = sbt("hbf", [128, TS * BN], BF16)
        carry = sbt("carry", [128, BN], F32)
        tmp32 = sbt("tmp32", [128, BN], F32)
        ao = sbt("ao", [128, NT_C, D], F32)
        yln = sbt("yln", [128, NT_G, D], BF16)
        ylnf = sbt("ylnf", [128, DC, GROWS], BF16)
        act = sbt("act", [128, FC, GROWS], BF16)
        fo = sbt("fo", [128, 2, D], F32)
        qbuf = sbt("qbuf", [128, GROWS], F32)
        sinb = sbt("sinb", [128, GROWS], F32)
        sq = sbt("sq", [128, D], F32)
        bwt = sbt("bwt_sb", [128, DC, D], BF16)
        wot = sbt("wot_sb", [128, DC, D], BF16)
        w1p = [sbt(f"w1p{i}", [128, 4, DC, 128], BF16) for i in range(2)]
        w2p = [sbt(f"w2p{i}", [128, 4, D], BF16) for i in range(2)]
        dB = sbt("dB", [128, BN], F32)
        d8 = sbt("d8", [128, DC], F32)
        phi = sbt("phi", [128, FC], F32)
        cpr = sbt("cpr", [128, FC], F32)
        b1f = sbt("b1f", [128, FC], F32)
        sbias = sbt("sbias", [128, FC], F32)
        th32 = sbt("th32", [128, FC], F32)
        wr32 = sbt("wr32", [128, FC], F32)
        wi32 = sbt("wi32", [128, FC], F32)
        tA = sbt("tA", [128, FC], F32)
        tBt = sbt("tB", [128, FC], F32)
        bo_bc = sbt("bo_bc", [128, D], F32)
        b2_bc = sbt("b2_bc", [128, D], F32)
        iden32 = sbt("iden32", [128, 128], F32)
        ident = sbt("ident", [128, 128], BF16)
        halfpi = sbt("halfpi", [128, 1], F32)
        dummy_sb = sbt("dummy_sb", [128, 64], F32)
        ones1 = sbt("ones1", [1, 128], F32)
        s_sum = sbt("s_sum", [128, NT_G], F32)
        s_sum3 = sbt("s_sum3", [128, NT_G], F32)
        s_sq = sbt("s_sq", [128, NT_G], F32)
        s_m = sbt("s_m", [128, NT_G], F32)
        s_v = sbt("s_v", [128, NT_G], F32)
        s_s = sbt("s_s", [128, NT_G], F32)
        s_q = sbt("s_q", [128, NT_G], F32)
        s_r = sbt("s_r", [128, NT_G], F32)
        s_mr = sbt("s_mr", [128, NT_G], F32)

        pbx = es.enter_context(nc.psum_tensor([128, 512], F32))
        ptr = es.enter_context(nc.psum_tensor([128, 512], F32))
        pwo = es.enter_context(nc.psum_tensor([128, 1024], F32))
        pffn = es.enter_context(nc.psum_tensor([128, 2048], F32))
        pw1 = [pffn.ap()[:, 0:512], pffn.ap()[:, 512:1024]]
        pw2 = [pffn.ap()[:, 1024:2048], pffn.ap()[:, 0:1024]]
        pw2h = [[pffn.ap()[:, 1024:1536], pffn.ap()[:, 1536:2048]],
                [pffn.ap()[:, 0:512], pffn.ap()[:, 512:1024]]]

        sch = Sched(nc, es)
        V, S_, T, Y, G = "vector", "scalar", "tensor", "sync", "gpsimd"

        def v(fn, incs=(), waits=()):
            sch.op(V, fn, incs=incs, waits=waits)

        def s_(fn, incs=(), waits=()):
            sch.op(S_, fn, incs=incs, waits=waits)

        def t_(fn, incs=(), waits=()):
            sch.op(T, fn, incs=incs, waits=waits)

        def vdrain():
            sch.op(V, "DRAIN")

        def vmargin(sem):
            sch.op(V, lambda e: e.tensor_copy(out=dummy_sb[:, 0:32],
                                              in_=dummy_sb[:, 32:64]),
                   incs=[(sem, 1)])

        def smargin(sem):
            sch.op(S_, lambda e: e.activation(out=dummy_sb[:, 0:32],
                                              in_=dummy_sb[:, 32:64],
                                              func=AF.Copy),
                   incs=[(sem, 1)])


        for dst, src in [
            (d8[:], a_h),
            (th32[:], th_h),
            (wr32[:], wr_h),
            (wi32[:], wi_h),
            (b1f[:], b1_h),
            (sq[0:1, 0:D], bo_h.rearrange("(o d) -> o d", o=1)),
        ]:
            sch.op(Y, lambda e, d=dst, s=src: e.dma_start(out=d, in_=s),
                   incs=[("CLD", 16)])
        sch.op(G, lambda e: e.dma_start(
            out=bwt[:], in_=bwt_h.rearrange("(c p) n -> p c n", p=128)),
            incs=[("W0L", 16)])
        sch.op(G, lambda e: e.dma_start(
            out=wot[:], in_=wot_h.rearrange("(c p) n -> p c n", p=128)),
            incs=[("W0L", 16)])
        sch.op(G, lambda e: e.dma_start(out=dummy_sb[:],
                                        in_=w1_h[0:128, 0:64]),
               incs=[("W0L", 16)])
        for pc in range(8):
            for k in range(4):
                sch.op(G, lambda e, pc=pc, k=k: e.dma_start(
                    out=w1scr[pc].rearrange("p (k c f) -> p k c f",
                                            k=4, c=DC)[:, k, :, :],
                    in_=w1_h.rearrange("(c p) (q k f) -> p q k c f",
                                       p=128, q=8, k=4)[:, pc, k, :, :]),
                    incs=[("SCW", 16)])
            sch.op(G, lambda e, pc=pc: e.dma_start(
                out=w2scr[pc].rearrange("p (k o) -> p k o", k=4),
                in_=w2_h.rearrange("(q k p) o -> p q k o", p=128, q=8)
                [:, pc, :, :]),
                incs=[("SCW", 16)])
        sch.op(G, lambda e: e.dma_start(out=dummy_sb[:],
                                        in_=w1_h[0:128, 0:64]),
               incs=[("SCW", 16)])
        v(lambda e: e.memset(ones1[:], 1.0), incs=[("VONE", 1)])
        for half in range(2):
            t_(lambda e, half=half: e.matmul(
                pwo.ap()[:, half * 512:(half + 1) * 512],
                ones1[:], sq[0:1, half * 512:(half + 1) * 512],
                start=True, stop=True),
               incs=[("PBC", 1)],
               waits=([("VONE", 1), ("CLD", 96)] if half == 0 else ()))
        sch.wait(V, ("PBC", 2))
        v(lambda e: e.tensor_copy(out=bo_bc[:], in_=pwo.ap()[:, 0:1024]),
          incs=[("BBC", 1)])
        sch.op(Y, lambda e: e.dma_start(
            out=sq[0:1, 0:D], in_=b2_h.rearrange("(o d) -> o d", o=1)),
            incs=[("CLD", 16)], waits=[("BBC", 1)])
        for half in range(2):
            t_(lambda e, half=half: e.matmul(
                pwo.ap()[:, half * 512:(half + 1) * 512],
                ones1[:], sq[0:1, half * 512:(half + 1) * 512],
                start=True, stop=True),
               incs=[("PBC", 1)],
               waits=([("CLD", 112), ("BBC", 1)] if half == 0 else ()))
        sch.wait(V, ("PBC", 4))
        v(lambda e: e.tensor_copy(out=b2_bc[:], in_=pwo.ap()[:, 0:1024]),
          incs=[("BBC", 1)])
        sch.op(G, lambda e: e.iota(iden32[:], pattern=[[-1, 128]], base=0,
                                   channel_multiplier=1,
                                   allow_small_or_imprecise_dtypes=True),
               incs=[("STG", 1)])

        sch.wait(V, "CLD", "STG")
        v(lambda e: e.tensor_single_scalar(out=ident[:], in_=iden32[:],
                                           scalar=0.0, op=ALU.is_equal),
          incs=[("VST", 1)])
        v(lambda e: e.tensor_single_scalar(out=d8[:], in_=d8[:], scalar=10.0,
                                           op=ALU.min), incs=[("VMIN", 1)])
        s_(lambda e: e.activation(out=d8[:], in_=d8[:], func=AF.Exp,
                                  scale=-1.0), incs=[("ST1", 1)],
           waits=["VMIN"])
        v(lambda e: e.tensor_copy(
            out=dB.ap().rearrange("p (nc b) -> p nc b", b=B),
            in_=d8.ap().rearrange("p (nc o) -> p nc o", o=1).broadcast_to(
                [128, DC, B])), waits=["ST1"])
        v(lambda e: e.memset(halfpi[:], PI / 2))
        v(lambda e: e.tensor_copy(out=phi[:], in_=th32[:]))
        for it in range(44):
            v(lambda e: e.tensor_mul(out=tA[:], in0=phi[:], in1=phi[:]))
            v(lambda e: e.tensor_single_scalar(out=tBt[:], in_=phi[:],
                                               scalar=0.0, op=ALU.is_ge))
            vdrain()
            v(lambda e: e.tensor_single_scalar(out=tA[:], in_=tA[:],
                                               scalar=(PI / 2) ** 2,
                                               op=ALU.is_gt))
            v(lambda e: e.tensor_scalar(out=tBt[:], in0=tBt[:],
                                        scalar1=2 * PI, scalar2=-PI,
                                        op0=ALU.mult, op1=ALU.add))
            vdrain()
            v(lambda e: e.tensor_mul(out=tA[:], in0=tA[:], in1=tBt[:]))
            vdrain()
            v(lambda e: e.tensor_sub(out=phi[:], in0=phi[:], in1=tA[:]),
              incs=([("ST2", 1)] if it == 43 else ()))
            vdrain()
        s_(lambda e: e.activation(out=tBt[:], in_=phi[:], func=AF.Sin),
           waits=["ST2"])
        s_(lambda e: e.activation(out=tA[:], in_=phi[:], func=AF.Sin,
                                  scale=-1.0, bias=halfpi[:]),
           incs=[("ST1", 1)])
        v(lambda e: e.tensor_mul(out=cpr[:], in0=tA[:], in1=wr32[:]),
          waits=["ST1"])
        v(lambda e: e.tensor_mul(out=tBt[:], in0=tBt[:], in1=wi32[:]))
        vdrain()
        v(lambda e: e.tensor_add(out=cpr[:], in0=cpr[:], in1=tBt[:]))
        vdrain()
        v(lambda e: e.tensor_mul(out=cpr[:], in0=cpr[:], in1=wi32[:]))
        vdrain()
        v(lambda e: e.tensor_single_scalar(out=cpr[:], in_=cpr[:], scalar=0.1,
                                           op=ALU.mult))
        v(lambda e: e.scalar_tensor_tensor(out=sbias[:], in0=b1f[:],
                                           scalar=0.1, in1=phi[:],
                                           op0=ALU.mult, op1=ALU.add))
        v(lambda e: e.memset(carry[:], 0.0), incs=[("CARC", 1)])
        vdrain()

        def ln_stats(nt0, nt, sums=None):
            if sums is None:
                sums = s_sum
            sl = np.s_[:, nt0:nt0 + nt]
            vdrain()
            v(lambda e, sums=sums: e.tensor_single_scalar(
                out=s_m[sl], in_=sums[sl], scalar=1.0 / D, op=ALU.mult))
            v(lambda e, sums=sums: e.tensor_mul(out=s_q[sl], in0=sums[sl],
                                                in1=sums[sl]))
            vdrain()
            v(lambda e: e.scalar_tensor_tensor(out=s_v[sl], in0=s_q[sl],
                                               scalar=-1.0 / D, in1=s_sq[sl],
                                               op0=ALU.mult, op1=ALU.add))
            vdrain()
            v(lambda e: e.tensor_scalar(out=s_v[sl], in0=s_v[sl],
                                        scalar1=0.5 / D, scalar2=0.5e-5,
                                        op0=ALU.mult, op1=ALU.add))
            vdrain()
            v(lambda e: e.tensor_single_scalar(out=s_s[sl], in_=s_v[sl],
                                               scalar=0.5, op=ALU.add))
            vdrain()
            for _ in range(NEWTON_ITERS):
                v(lambda e: e.reciprocal(out=s_q[sl], in_=s_s[sl]))
                vdrain()
                v(lambda e: e.tensor_mul(out=s_q[sl], in0=s_v[sl],
                                         in1=s_q[sl]))
                vdrain()
                v(lambda e: e.scalar_tensor_tensor(out=s_s[sl], in0=s_s[sl],
                                                   scalar=0.5, in1=s_q[sl],
                                                   op0=ALU.mult, op1=ALU.add))
                vdrain()
            v(lambda e: e.reciprocal(out=s_r[sl], in_=s_s[sl]))
            vdrain()
            v(lambda e: e.tensor_mul(out=s_mr[sl], in0=s_m[sl], in1=s_r[sl]))
            vdrain()

        sch.op(S_, None, waits=[("PBC", 4)])
        n_out_box = [0]

        def x_load(gg):
            buf0 = xy[gg % 2]
            for cc in range(GC):
                c = gg * GC + cc
                for j in range(NT_C):
                    sch.op(Y, lambda e, c=c, cc=cc, j=j, buf=buf0:
                           e.dma_start(
                            out=buf[:, cc * NT_C + j, :],
                            in_=x_h[2 * j:2 * j + 2,
                                    c * TS:(c + 1) * TS, :]),
                           incs=[("XLD", 16)])

        def attn_a(g, cc):
            gx = xy[g % 2]
            c = g * GC + cc
            nt0 = cc * NT_C
            for j in range(NT_C):
                xt = gx[:, nt0 + j, :]
                v(lambda e, xt=xt, nj=nt0 + j: e.reduce_sum(
                    out=s_sum[:, nj:nj + 1], in_=xt, axis=AX.X),
                  waits=(["XLD"] if j == 0 else ()))
                s_(lambda e, xt=xt, nj=nt0 + j: e.activation(
                    out=sq[:], in_=xt, func=AF.Square,
                    accum_out=s_sq[:, nj:nj + 1]),
                   incs=[("LN1S", 1)], waits=(["XLD"] if j == 0 else ()))
            sch.wait(V, "LN1S")
            ln_stats(nt0, NT_C)
            for j in range(NT_C):
                xt = gx[:, nt0 + j, :]
                v(lambda e, xt=xt, j=j, nj=nt0 + j: e.tensor_scalar(
                    out=xln[:, j, :], in0=xt,
                    scalar1=s_m[:, nj:nj + 1], scalar2=s_r[:, nj:nj + 1],
                    op0=ALU.subtract, op1=ALU.mult),
                  waits=(["TRX"] if j == 0 else ()))
            vmargin("LN1V")
            sch.wait(T, "LN1V", ("VST", 1), ("W0L", 48), ("BBC", 2))
            for ic in range(DC):
                for j in range(NT_C):
                    t_(lambda e, ic=ic, j=j: e.matmul(
                        ptr[0:128, j * 128:(j + 1) * 128],
                        xln[:, j, ic * 128:(ic + 1) * 128], ident[:],
                        start=True, stop=True),
                       incs=([("TRX", 1)] if j == NT_C - 1 else ()),
                       waits=(["EVT"] if j == 0 else ()))
                s_(lambda e, ic=ic: e.activation(
                    out=xlnf[:, ic, :], in_=ptr[0:128, 0:ROWS],
                    func=AF.Copy), incs=[("EVT", 1)], waits=["TRX"])
            smargin("EVT")
            for ncc in range(DC):
                for ic in range(DC):
                    t_(lambda e, ncc=ncc, ic=ic: e.matmul(
                        pbx[0:128, 0:ROWS],
                        bwt[:, ic, ncc * 128:(ncc + 1) * 128],
                        xlnf[:, ic, :],
                        start=(ic == 0), stop=(ic == DC - 1)),
                       incs=([("BX", 1)] if ic == DC - 1 else ()),
                       waits=((["EVT", "BXE"]) if ic == 0 else ()))
                v(lambda e, ncc=ncc: e.tensor_copy(
                    out=bxb.ap().rearrange(
                        "p (s nc j b1) -> p nc j b1 s", nc=DC, j=2, b1=2)
                    [:, ncc, :, :, :],
                    in_=pbx.ap()[0:128, 0:ROWS].rearrange(
                        "p (j b1 s) -> p j b1 s", j=2, b1=2)),
                  incs=[("BXE", 1)], waits=["BX"])

        def scan_chunk(g, cc, epi=None):
            gx = xy[g % 2]
            nt0 = cc * NT_C
            fcs = list(epi[1]) if epi else []
            sch.wait(G, "BXE")
            for t in range(TS):
                prev = carry[:] if t == 0 else hbuf[:, (t - 1) * BN:t * BN]
                sch.op(G, lambda e, prev=prev: e.tensor_mul(
                    out=tmp32[:], in0=prev, in1=dB[:]),
                  waits=(["SCS"] if t > 0 else ["CARC"]))
                sch.op(G, lambda e, t=t: e.tensor_add(
                    out=tmp32[:], in0=tmp32[:],
                    in1=bxb[:, t * BN:(t + 1) * BN]), incs=[("SCV", 1)])
                s_(lambda e, t=t: e.activation(
                    out=hbuf[:, t * BN:(t + 1) * BN], in_=tmp32[:],
                    func=AF.Tanh), incs=[("SCS", 1)], waits=["SCV"])
                if fcs and t >= 12 and t % 3 == 0:
                    ffn_w1_VS(epi[0], fcs.pop(0))
            while fcs:
                ffn_w1_VS(epi[0], fcs.pop(0))
            sch.wait(V, "SCS")
            v(lambda e: e.tensor_copy(out=carry[:],
                                      in_=hbuf[:, (TS - 1) * BN:]),
              incs=[("CARC", 1)])
            for ncc in range(DC):
                v(lambda e, ncc=ncc: e.tensor_copy(
                    out=hbf[:, ncc * ROWS:(ncc + 1) * ROWS],
                    in_=hbuf.ap().rearrange(
                        "p (s nc j b1) -> p nc j b1 s", nc=DC, j=2, b1=2)
                    [:, ncc, :, :, :]),
                  waits=(["WO"] if ncc == 0 else ()))
            vmargin("HCA")

        def wo_ln2(g, cc):
            gx = xy[g % 2]
            nt0 = cc * NT_C
            sch.wait(T, "HCA")
            for j in range(NT_C):
                for ncc in range(DC):
                    lap = hbf[:, (ncc * 2 + j) * 128:
                              (ncc * 2 + j) * 128 + 128]
                    for half in range(2):
                        t_(lambda e, ncc=ncc, half=half, lap=lap: e.matmul(
                            pwo[0:128, half * 512:(half + 1) * 512],
                            lap,
                            wot[:, ncc, half * 512:(half + 1) * 512],
                            start=(ncc == 0), stop=(ncc == DC - 1)),
                           incs=([("WO", 1)] if (ncc == DC - 1 and
                                                 half == 1) else ()),
                           waits=(["LN2V"] if (ncc == 0 and half == 0)
                                  else ()))
                sch.wait(V, "WO")
                v(lambda e, j=j, nj=nt0 + j: e.scalar_tensor_tensor(
                    out=ao[:, j, :], in0=pwo[0:128, 0:1024], scalar=0.0,
                    in1=bo_bc[:], op0=ALU.bypass, op1=ALU.add,
                    accum_out=s_sum[:, nj:nj + 1]), incs=[("LN2V", 1)])
                s_(lambda e, j=j, nj=nt0 + j: e.activation(
                    out=sq[:], in_=ao[:, j, :], func=AF.Square,
                    accum_out=s_sq[:, nj:nj + 1]),
                   incs=[("LN2S", 1)], waits=["LN2V"])
            sch.wait(V, "LN2S")
            ln_stats(nt0, NT_C)
            for j in range(NT_C):
                xt = gx[:, nt0 + j, :]
                v(lambda e, xt=xt, nj=nt0 + j: e.tensor_scalar(
                    out=xt, in0=xt, scalar1=s_mr[:, nj:nj + 1],
                    scalar2=None, op0=ALU.subtract))
                vdrain()
                v(lambda e, xt=xt, j=j, nj=nt0 + j: e.scalar_tensor_tensor(
                    out=xt, in0=ao[:, j, :],
                    scalar=s_r[:, nj:nj + 1], in1=xt,
                    op0=ALU.mult, op1=ALU.add,
                    accum_out=s_sum3[:, nj:nj + 1]), incs=[("YST", 1)])

        def ln3_try(g):
            gx = xy[g % 2]
            for j in range(NT_G):
                s_(lambda e, j=j, buf=gx: e.activation(
                    out=sq[:], in_=buf[:, j, :], func=AF.Square,
                    accum_out=s_sq[:, j:j + 1]),
                   incs=[("LN3S", 1)], waits=(["YST"] if j == 0 else ()))
            sch.wait(V, "LN3S")
            ln_stats(0, NT_G, sums=s_sum3)
            for j in range(NT_G):
                v(lambda e, j=j, buf=gx: e.tensor_scalar(
                    out=yln[:, j, :], in0=buf[:, j, :],
                    scalar1=s_m[:, j:j + 1], scalar2=s_r[:, j:j + 1],
                    op0=ALU.subtract, op1=ALU.mult),
                  waits=(["TRY"] if j == 0 else ()))
            vmargin("LN3V")
            sch.wait(T, "LN3V")
            for ic in range(DC):
                for j in range(NT_G):
                    t_(lambda e, ic=ic, j=j: e.matmul(
                        ptr[0:128, j * 128:(j + 1) * 128],
                        yln[:, j, ic * 128:(ic + 1) * 128], ident[:],
                        start=True, stop=True),
                       incs=([("TRY", 1)] if j == NT_G - 1 else ()),
                       waits=(["EVT"] if j == 0 else ()))
                s_(lambda e, ic=ic: e.activation(
                    out=ylnf[:, ic, :], in_=ptr[0:128, 0:GROWS],
                    func=AF.Copy), incs=[("EVT", 1)], waits=["TRY"])
            smargin("EVT")

        def ffn_w1_T(g, lo, hi):
            def w1load(pc, waits=()):
                sch.op(Y, lambda e, pc=pc: e.dma_start(
                    out=w1p[pc % 2].ap().rearrange("p k c f -> p (k c f)"),
                    in_=w1scr[pc]),
                    incs=[("W1L", 16)], waits=waits)

            if lo == 0:
                for pc in range(2):
                    w = [("SCW", 16 * 41)]
                    if g > 0:
                        w.append(("W1M", 32 * g - 32 + 4 * (pc + 7)))
                    w1load(pc, waits=w)
                sch.op(T, None, waits=[("EVT", 27 * (g + 1))])
            for fc in range(lo, hi):
                pb = pw1[fc % 2]
                tw = [("W1L", 16 * (9 * g + min(fc // 4 + 2, 9)))]
                if fc >= 2:
                    tw.append(("QV", 32 * g + fc - 1))
                else:
                    tw.append(("FOE", 4 * g))
                for ic in range(DC):
                    t_(lambda e, fc=fc, ic=ic, pb=pb: e.matmul(
                        pb, w1p[(fc // 4) % 2][:, fc % 4, ic, :],
                        ylnf[:, ic, :],
                        start=(ic == 0), stop=(ic == DC - 1)),
                       incs=([("W1M", 1)] if ic == DC - 1 else ()),
                       waits=(tw if ic == 0 else ()))
                if fc % 4 == 3 and fc // 4 + 2 < 8:
                    w1load(fc // 4 + 2, waits=[("W1M", 32 * g + fc + 1)])
                if fc == FC - 1:
                    sch.op(Y, lambda e: e.dma_start(
                        out=dummy_sb[:], in_=w1_h[0:128, 0:64]),
                        incs=[("W1L", 16)])

        def ffn_w1_VS(g, fc):
            pb = pw1[fc % 2]
            s_(lambda e, fc=fc, pb=pb: e.activation(
                out=sinb[:], in_=pb, func=AF.Sin,
                scale=0.1, bias=sbias[:, fc:fc + 1]),
               incs=[("SIN", 1)],
               waits=[("W1M", 32 * g + fc + 1), ("QV", 32 * g + fc)])
            v(lambda e, fc=fc, pb=pb: e.scalar_tensor_tensor(
                out=qbuf[:], in0=sinb[:], scalar=cpr[:, fc:fc + 1],
                in1=pb, op0=ALU.mult, op1=ALU.add),
              incs=[("QV", 1)],
              waits=[("SIN", 32 * g + fc + 1), ("ACT", 33 * g + fc)])
            v(lambda e, fc=fc: e.tensor_scalar(
                out=act[:, fc, :], in0=qbuf[:],
                scalar1=b1f[:, fc:fc + 1], scalar2=0.0,
                op0=ALU.add, op1=ALU.max), incs=[("ACT", 1)],
              waits=[("QV", 32 * g + fc + 1)] +
              ([("W2F", 64 * g)] if fc == 0 else []))
            if fc == FC - 1:
                vmargin("ACT")

        def ffn_w2_T(g, jps):
            def w2load(pc, P, waits=()):
                sch.op(Y, lambda e, pc=pc: e.dma_start(
                    out=w2p[pc % 2].ap().rearrange("p k o -> p (k o)"),
                    in_=w2scr[pc]),
                    incs=[("W2L", 16)], waits=waits)

            for jp in jps:
                P = 2 * g + jp
                for pa, (jj, sem) in enumerate([(0, "W2A"), (1, "W2F")]):
                    j = 2 * jp + jj
                    base = 16 * (18 * P + 9 * pa)
                    for pc in range(2):
                        if pa == 0:
                            w = [("SCW", 16 * 41)]
                            if P > 0:
                                w.append(("W2F", 32 * P - 4 + 4 * pc))
                        else:
                            w = [("W2A", 32 * (P + 1))]
                        w2load(pc, P, waits=w)
                    if pa == 0:
                        sch.op(T, None, waits=[("FOE", 4 * g + 2 * jp)])
                    else:
                        sch.op(T, None, waits=[("QV", 32 * (g + 1)),
                                               ("ACT", 33 * (g + 1))])
                    for fc in range(FC):
                        for half in range(2):
                            tw = []
                            if half == 0:
                                tw.append(("W2L", base +
                                           16 * min(fc // 4 + 2, 9)))
                                if pa == 0:
                                    tw.append(("ACT", 33 * g + fc + 1))
                            t_(lambda e, fc=fc, half=half, j=j, jj=jj:
                               e.matmul(
                                pw2h[jj][half],
                                act[:, fc, j * 128:(j + 1) * 128],
                                w2p[(fc // 4) % 2][:, fc % 4,
                                    half * 512:(half + 1) * 512],
                                start=(fc == 0), stop=(fc == FC - 1)),
                               incs=([(sem, 1)] if half == 1 else ()),
                               waits=tw)
                        if fc % 4 == 3 and fc // 4 + 2 < 8:
                            w2load(fc // 4 + 2, P,
                                   waits=[(sem, 32 * P + fc + 1)])
                        if fc == FC - 1:
                            sch.op(Y, lambda e: e.dma_start(
                                out=dummy_sb[:], in_=w1_h[0:128, 0:64]),
                                incs=[("W2L", 16)])

        def ffn_w2_VS(g, jps):
            gx = xy[g % 2]
            for jp in jps:
                P = 2 * g + jp
                sch.op(V, None, waits=[("W2F", 32 * (P + 1))])
                for jj in range(2):
                    j = 2 * jp + jj
                    v(lambda e, jj=jj, j=j: e.scalar_tensor_tensor(
                        out=fo[:, jj, :], in0=pw2[jj], scalar=0.0,
                        in1=b2_bc[:], op0=ALU.bypass, op1=ALU.add,
                        accum_out=s_sum[:, j:j + 1]), incs=[("FOE", 1)])
                    s_(lambda e, jj=jj, j=j: e.activation(
                        out=sq[:], in_=fo[:, jj, :], func=AF.Square,
                        accum_out=s_sq[:, j:j + 1]),
                       incs=[("LN4S", 1)], waits=["FOE"])
                sch.wait(V, "LN4S")
                ln_stats(2 * jp, 2)
                for jj in range(2):
                    j = 2 * jp + jj
                    xt = gx[:, j, :]
                    v(lambda e, xt=xt, j=j: e.tensor_scalar(
                        out=xt, in0=xt, scalar1=s_mr[:, j:j + 1],
                        scalar2=None, op0=ALU.subtract))
                    vdrain()
                    v(lambda e, xt=xt, j=j, jj=jj: e.scalar_tensor_tensor(
                        out=xt, in0=fo[:, jj, :], scalar=s_r[:, j:j + 1],
                        in1=xt, op0=ALU.mult, op1=ALU.add),
                      incs=[("LN4V", 1)])
            jp = jps[-1]
            cc = jp
            c = g * GC + cc
            sch.wait(Y, "LN4V")
            for j in range(NT_C):
                sch.op(Y, lambda e, c=c, cc=cc, j=j, buf=gx: e.dma_start(
                    out=o_h[2 * j:2 * j + 2, c * TS:(c + 1) * TS, :],
                    in_=buf[:, cc * NT_C + j, :]),
                    incs=[("OUT", 16)])
                n_out_box[0] += 1
            if g + 2 < NG:
                sch.op(Y, None, waits=[("OUT", 16 * n_out_box[0])])
                c2 = (g + 2) * GC + cc
                for j in range(NT_C):
                    sch.op(Y, lambda e, c2=c2, cc=cc, j=j, buf=gx:
                           e.dma_start(
                            out=buf[:, cc * NT_C + j, :],
                            in_=x_h[2 * j:2 * j + 2,
                                    c2 * TS:(c2 + 1) * TS, :]),
                           incs=[("XLD", 16)])

        x_load(0)
        x_load(1)
        attn_a(0, 0)
        for g in range(NG):
            if g > 0:
                ffn_w1_T(g - 1, 0, 16)
            scan_chunk(g, 0,
                       epi=((g - 1, range(0, 16)) if g > 0 else None))
            attn_a(g, 1)
            wo_ln2(g, 0)
            if g > 0:
                ffn_w1_T(g - 1, 16, 32)
                ffn_w2_T(g - 1, [0])
            scan_chunk(g, 1,
                       epi=((g - 1, range(16, 32)) if g > 0 else None))
            if g > 0:
                ffn_w2_VS(g - 1, [0])
            if g + 1 < NG:
                attn_a(g + 1, 0)
            wo_ln2(g, 1)
            ln3_try(g)
            if g > 0:
                ffn_w2_T(g - 1, [1])
                ffn_w2_VS(g - 1, [1])
        glast = NG - 1
        ffn_w1_T(glast, 0, 32)
        for fc in range(FC):
            ffn_w1_VS(glast, fc)
        ffn_w2_T(glast, [0])
        ffn_w2_VS(glast, [0])
        ffn_w2_T(glast, [1])
        ffn_w2_VS(glast, [1])

        sch.op(Y, None, waits=[("OUT", 16 * n_out_box[0])])

        with nc.Block() as block:
            sch.replay(block)

    return nc


_NC_CACHE = None


def kernel(L0_attn_x, L0_attn_A, L0_attn_B_w, L0_attn_Wo, L0_attn_bo,
           L0_ffn_W1, L0_ffn_b1, L0_ffn_quantum_phases,
           L0_ffn_quantum_weights_real, L0_ffn_quantum_weights_imag,
           L0_ffn_W2, L0_ffn_b2):
    global _NC_CACHE
    x = np.ascontiguousarray(np.asarray(L0_attn_x, dtype=np.float32))
    shared = {
        "bwt": np.ascontiguousarray(np.asarray(L0_attn_B_w, np.float32).T),
        "wot": np.ascontiguousarray(np.asarray(L0_attn_Wo, np.float32).T),
        "w1": np.ascontiguousarray(np.asarray(L0_ffn_W1, np.float32)),
        "w2": np.ascontiguousarray(np.asarray(L0_ffn_W2, np.float32)),
        "a_dec": np.ascontiguousarray(
            np.asarray(L0_attn_A, np.float32).reshape(-1, 128).T),
        "bo": np.ascontiguousarray(np.asarray(L0_attn_bo, np.float32)),
        "b1": np.ascontiguousarray(
            np.asarray(L0_ffn_b1, np.float32).reshape(-1, 128).T),
        "theta": np.ascontiguousarray(
            np.asarray(L0_ffn_quantum_phases, np.float32).reshape(-1, 128).T),
        "wre": np.ascontiguousarray(
            np.asarray(L0_ffn_quantum_weights_real,
                       np.float32).reshape(-1, 128).T),
        "wim": np.ascontiguousarray(
            np.asarray(L0_ffn_quantum_weights_imag,
                       np.float32).reshape(-1, 128).T),
        "b2": np.ascontiguousarray(np.asarray(L0_ffn_b2, np.float32)),
    }
    if _NC_CACHE is None:
        _NC_CACHE = build_nc()
    nc = _NC_CACHE
    in_maps = []
    for k in range(NCORES):
        m = dict(shared)
        m["x"] = np.ascontiguousarray(x[k * B:(k + 1) * B])
        in_maps.append(m)
    res = run_bass_kernel_spmd(nc, in_maps, core_ids=list(range(NCORES)))
    return np.concatenate([r["o"] for r in res.results], axis=0)


# version 20 (final)
# speedup vs baseline: 1.2956x; 1.2956x over previous
import contextlib
import numpy as np

import concourse.bass as bass
import concourse.mybir as mybir
from concourse.alu_op_type import AluOpType as ALU
from concourse.bass_utils import run_bass_kernel_spmd

F32 = mybir.dt.float32
BF16 = mybir.dt.bfloat16
AF = mybir.ActivationFunctionType
AX = mybir.AxisListType

NCORES = 8
B_TOT, S_TOT, D, F = 32, 2048, 1024, 4096
B = B_TOT // NCORES
TS = 64
ROWS = B * TS
NT_C = ROWS // 128
GC = 2
GROWS = ROWS * GC
NT_G = GROWS // 128
NCH = S_TOT // TS
NG = NCH // GC
DC = D // 128
FC = F // 128
BN = DC * B
NEWTON_ITERS = 7
PI = float(np.pi)
PARANOID = False


class Sched:

    def __init__(self, nc, es):
        self.nc = nc
        self.es = es
        self.events = {"tensor": [], "vector": [], "scalar": [], "sync": [],
                       "gpsimd": []}
        self.sems = {}
        self.counts = {}

    def sem(self, name):
        if name not in self.sems:
            self.sems[name] = self.es.enter_context(self.nc.semaphore(name))
            self.counts[name] = 0
        return self.sems[name]

    def c(self, name):
        self.sem(name)
        return self.counts[name]

    def op(self, eng, fn, incs=(), waits=()):
        wl = []
        for w in waits:
            if isinstance(w, str):
                self.sem(w)
                wl.append((w, self.counts[w]))
            else:
                self.sem(w[0])
                wl.append((w[0], w[1]))
        il = []
        for s, k in incs:
            self.sem(s)
            self.counts[s] += k
            il.append((s, k))
        self.events[eng].append((fn, wl, il))

    def wait(self, eng, *sem_names):
        self.op(eng, None, waits=list(sem_names))

    def replay(self, block):
        def mk(eng_name):
            def body(eng):
                for fn, wl, il in self.events[eng_name]:
                    for s, v in wl:
                        if v > 0:
                            eng.wait_ge(self.sems[s], v)
                    if fn is None:
                        continue
                    if fn == "DRAIN":
                        eng.drain()
                        continue
                    inst = fn(eng)
                    for s, k in il:
                        inst.then_inc(self.sems[s], k)
            return body
        block.tensor(mk("tensor"))
        block.vector(mk("vector"))
        block.scalar(mk("scalar"))
        block.sync(mk("sync"))
        block.gpsimd(mk("gpsimd"))


def build_nc():
    nc = bass.Bass("TRN2", target_bir_lowering=False,
                   detect_race_conditions=False)
    es = contextlib.ExitStack()

    x_h = nc.dram_tensor("x", [B, S_TOT, D], F32, kind="ExternalInput").ap()
    bwt_h = nc.dram_tensor("bwt", [D, D], F32, kind="ExternalInput").ap()
    wot_h = nc.dram_tensor("wot", [D, D], F32, kind="ExternalInput").ap()
    w1_h = nc.dram_tensor("w1", [D, F], F32, kind="ExternalInput").ap()
    w2_h = nc.dram_tensor("w2", [F, D], F32, kind="ExternalInput").ap()
    a_h = nc.dram_tensor("a_dec", [128, D // 128], F32, kind="ExternalInput").ap()
    bo_h = nc.dram_tensor("bo", [D], F32, kind="ExternalInput").ap()
    b1_h = nc.dram_tensor("b1", [128, F // 128], F32, kind="ExternalInput").ap()
    th_h = nc.dram_tensor("theta", [128, F // 128], F32, kind="ExternalInput").ap()
    wr_h = nc.dram_tensor("wre", [128, F // 128], F32, kind="ExternalInput").ap()
    wi_h = nc.dram_tensor("wim", [128, F // 128], F32, kind="ExternalInput").ap()
    b2_h = nc.dram_tensor("b2", [D], F32, kind="ExternalInput").ap()
    o_h = nc.dram_tensor("o", [B, S_TOT, D], F32, kind="ExternalOutput").ap()
    w1scr = nc.dram_tensor("w1scr", [8, 128, 4 * DC * 128], BF16,
                           kind="Internal").ap()
    w2scr = nc.dram_tensor("w2scr", [8, 128, 4 * D], BF16,
                           kind="Internal").ap()

    def sbt(name, shape, dtype, **kw):
        return es.enter_context(nc.sbuf_tensor(name, shape, dtype, **kw))

    with es:
        xy = [sbt(f"xy{i}", [128, NT_G, D], F32) for i in range(2)]
        xln = sbt("xln", [128, NT_C, D], BF16)
        xlnf = sbt("xlnf", [128, DC, ROWS], BF16)
        bxb = sbt("bxb", [128, TS * BN], F32)
        hbuf = sbt("hbuf", [128, TS * BN], F32)
        hbf = sbt("hbf", [128, TS * BN], BF16)
        carry = sbt("carry", [128, BN], F32)
        tmp32 = sbt("tmp32", [128, BN], F32)
        ao = sbt("ao", [128, NT_C, D], F32)
        yln = sbt("yln", [128, NT_G, D], BF16)
        ylnf = sbt("ylnf", [128, DC, GROWS], BF16)
        act = sbt("act", [128, FC, GROWS], BF16)
        fo = sbt("fo", [128, 2, D], F32)
        qbuf = sbt("qbuf", [128, GROWS], F32)
        sinb = sbt("sinb", [128, GROWS], F32)
        sq = sbt("sq", [128, D], F32)
        bwt = sbt("bwt_sb", [128, DC, D], BF16)
        wot = sbt("wot_sb", [128, DC, D], BF16)
        w1p = [sbt(f"w1p{i}", [128, 4, DC, 128], BF16) for i in range(2)]
        w2p = [sbt(f"w2p{i}", [128, 4, D], BF16) for i in range(2)]
        dB = sbt("dB", [128, BN], F32)
        d8 = sbt("d8", [128, DC], F32)
        phi = sbt("phi", [128, FC], F32)
        cpr = sbt("cpr", [128, FC], F32)
        b1f = sbt("b1f", [128, FC], F32)
        sbias = sbt("sbias", [128, FC], F32)
        th32 = sbt("th32", [128, FC], F32)
        wr32 = sbt("wr32", [128, FC], F32)
        wi32 = sbt("wi32", [128, FC], F32)
        tA = sbt("tA", [128, FC], F32)
        tBt = sbt("tB", [128, FC], F32)
        bo_bc = sbt("bo_bc", [128, D], F32)
        b2_bc = sbt("b2_bc", [128, D], F32)
        iden32 = sbt("iden32", [128, 128], F32)
        ident = sbt("ident", [128, 128], BF16)
        halfpi = sbt("halfpi", [128, 1], F32)
        epsb = sbt("epsb", [128, 1], F32)
        dummy_sb = sbt("dummy_sb", [128, 64], F32)
        ones1 = sbt("ones1", [1, 128], F32)
        s_sum = sbt("s_sum", [128, NT_G], F32)
        s_sum3 = sbt("s_sum3", [128, NT_G], F32)
        s_sq = sbt("s_sq", [128, NT_G], F32)
        s_m = sbt("s_m", [128, NT_G], F32)
        s_v = sbt("s_v", [128, NT_G], F32)
        s_s = sbt("s_s", [128, NT_G], F32)
        s_q = sbt("s_q", [128, NT_G], F32)
        s_r = sbt("s_r", [128, NT_G], F32)
        s_mr = sbt("s_mr", [128, NT_G], F32)

        pbx = es.enter_context(nc.psum_tensor([128, 512], F32))
        ptr = es.enter_context(nc.psum_tensor([128, 512], F32))
        pwo = es.enter_context(nc.psum_tensor([128, 1024], F32))
        pffn = es.enter_context(nc.psum_tensor([128, 2048], F32))
        pw1 = [pffn.ap()[:, 0:512], pffn.ap()[:, 512:1024]]
        pw2 = [pffn.ap()[:, 1024:2048], pffn.ap()[:, 0:1024]]
        pw2h = [[pffn.ap()[:, 1024:1536], pffn.ap()[:, 1536:2048]],
                [pffn.ap()[:, 0:512], pffn.ap()[:, 512:1024]]]

        sch = Sched(nc, es)
        V, S_, T, Y, G = "vector", "scalar", "tensor", "sync", "gpsimd"

        def v(fn, incs=(), waits=()):
            sch.op(V, fn, incs=incs, waits=waits)

        def s_(fn, incs=(), waits=()):
            sch.op(S_, fn, incs=incs, waits=waits)

        def t_(fn, incs=(), waits=()):
            sch.op(T, fn, incs=incs, waits=waits)

        def vdrain():
            sch.op(V, "DRAIN")

        def vmargin(sem):
            sch.op(V, lambda e: e.tensor_copy(out=dummy_sb[:, 0:32],
                                              in_=dummy_sb[:, 32:64]),
                   incs=[(sem, 1)])

        def smargin(sem):
            sch.op(S_, lambda e: e.activation(out=dummy_sb[:, 0:32],
                                              in_=dummy_sb[:, 32:64],
                                              func=AF.Copy),
                   incs=[(sem, 1)])


        for dst, src in [
            (d8[:], a_h),
            (th32[:], th_h),
            (wr32[:], wr_h),
            (wi32[:], wi_h),
            (b1f[:], b1_h),
            (sq[0:1, 0:D], bo_h.rearrange("(o d) -> o d", o=1)),
        ]:
            sch.op(Y, lambda e, d=dst, s=src: e.dma_start(out=d, in_=s),
                   incs=[("CLD", 16)])
        sch.op(G, lambda e: e.dma_start(
            out=bwt[:], in_=bwt_h.rearrange("(c p) n -> p c n", p=128)),
            incs=[("W0L", 16)])
        sch.op(G, lambda e: e.dma_start(
            out=wot[:], in_=wot_h.rearrange("(c p) n -> p c n", p=128)),
            incs=[("W0L", 16)])
        sch.op(G, lambda e: e.dma_start(out=dummy_sb[:],
                                        in_=w1_h[0:128, 0:64]),
               incs=[("W0L", 16)])
        for pc in range(8):
            for k in range(4):
                sch.op(G, lambda e, pc=pc, k=k: e.dma_start(
                    out=w1scr[pc].rearrange("p (k c f) -> p k c f",
                                            k=4, c=DC)[:, k, :, :],
                    in_=w1_h.rearrange("(c p) (q k f) -> p q k c f",
                                       p=128, q=8, k=4)[:, pc, k, :, :]),
                    incs=[("SCW", 16)])
            sch.op(G, lambda e, pc=pc: e.dma_start(
                out=w2scr[pc].rearrange("p (k o) -> p k o", k=4),
                in_=w2_h.rearrange("(q k p) o -> p q k o", p=128, q=8)
                [:, pc, :, :]),
                incs=[("SCW", 16)])
        sch.op(G, lambda e: e.dma_start(out=dummy_sb[:],
                                        in_=w1_h[0:128, 0:64]),
               incs=[("SCW", 16)])
        v(lambda e: e.memset(ones1[:], 1.0), incs=[("VONE", 1)])
        for half in range(2):
            t_(lambda e, half=half: e.matmul(
                pwo.ap()[:, half * 512:(half + 1) * 512],
                ones1[:], sq[0:1, half * 512:(half + 1) * 512],
                start=True, stop=True),
               incs=[("PBC", 1)],
               waits=([("VONE", 1), ("CLD", 96)] if half == 0 else ()))
        sch.wait(V, ("PBC", 2))
        v(lambda e: e.tensor_copy(out=bo_bc[:], in_=pwo.ap()[:, 0:1024]),
          incs=[("BBC", 1)])
        sch.op(Y, lambda e: e.dma_start(
            out=sq[0:1, 0:D], in_=b2_h.rearrange("(o d) -> o d", o=1)),
            incs=[("CLD", 16)], waits=[("BBC", 1)])
        for half in range(2):
            t_(lambda e, half=half: e.matmul(
                pwo.ap()[:, half * 512:(half + 1) * 512],
                ones1[:], sq[0:1, half * 512:(half + 1) * 512],
                start=True, stop=True),
               incs=[("PBC", 1)],
               waits=([("CLD", 112), ("BBC", 1)] if half == 0 else ()))
        sch.wait(V, ("PBC", 4))
        v(lambda e: e.tensor_copy(out=b2_bc[:], in_=pwo.ap()[:, 0:1024]),
          incs=[("BBC", 1)])
        sch.op(G, lambda e: e.iota(iden32[:], pattern=[[-1, 128]], base=0,
                                   channel_multiplier=1,
                                   allow_small_or_imprecise_dtypes=True),
               incs=[("STG", 1)])

        sch.wait(V, "CLD", "STG")
        v(lambda e: e.tensor_single_scalar(out=ident[:], in_=iden32[:],
                                           scalar=0.0, op=ALU.is_equal),
          incs=[("VST", 1)])
        v(lambda e: e.tensor_single_scalar(out=d8[:], in_=d8[:], scalar=10.0,
                                           op=ALU.min), incs=[("VMIN", 1)])
        s_(lambda e: e.activation(out=d8[:], in_=d8[:], func=AF.Exp,
                                  scale=-1.0), incs=[("ST1", 1)],
           waits=["VMIN"])
        v(lambda e: e.tensor_copy(
            out=dB.ap().rearrange("p (nc b) -> p nc b", b=B),
            in_=d8.ap().rearrange("p (nc o) -> p nc o", o=1).broadcast_to(
                [128, DC, B])), waits=["ST1"])
        v(lambda e: e.memset(halfpi[:], PI / 2))
        v(lambda e: e.memset(epsb[:], 1e-5))
        v(lambda e: e.tensor_copy(out=phi[:], in_=th32[:]))
        for it in range(44):
            v(lambda e: e.tensor_mul(out=tA[:], in0=phi[:], in1=phi[:]))
            v(lambda e: e.tensor_single_scalar(out=tBt[:], in_=phi[:],
                                               scalar=0.0, op=ALU.is_ge))
            vdrain()
            v(lambda e: e.tensor_single_scalar(out=tA[:], in_=tA[:],
                                               scalar=(PI / 2) ** 2,
                                               op=ALU.is_gt))
            v(lambda e: e.tensor_scalar(out=tBt[:], in0=tBt[:],
                                        scalar1=2 * PI, scalar2=-PI,
                                        op0=ALU.mult, op1=ALU.add))
            vdrain()
            v(lambda e: e.tensor_mul(out=tA[:], in0=tA[:], in1=tBt[:]))
            vdrain()
            v(lambda e: e.tensor_sub(out=phi[:], in0=phi[:], in1=tA[:]),
              incs=([("ST2", 1)] if it == 43 else ()))
            vdrain()
        s_(lambda e: e.activation(out=tBt[:], in_=phi[:], func=AF.Sin),
           waits=["ST2"])
        s_(lambda e: e.activation(out=tA[:], in_=phi[:], func=AF.Sin,
                                  scale=-1.0, bias=halfpi[:]),
           incs=[("ST1", 1)])
        v(lambda e: e.tensor_mul(out=cpr[:], in0=tA[:], in1=wr32[:]),
          waits=["ST1"])
        v(lambda e: e.tensor_mul(out=tBt[:], in0=tBt[:], in1=wi32[:]))
        vdrain()
        v(lambda e: e.tensor_add(out=cpr[:], in0=cpr[:], in1=tBt[:]))
        vdrain()
        v(lambda e: e.tensor_mul(out=cpr[:], in0=cpr[:], in1=wi32[:]))
        vdrain()
        v(lambda e: e.tensor_single_scalar(out=cpr[:], in_=cpr[:], scalar=0.1,
                                           op=ALU.mult))
        v(lambda e: e.scalar_tensor_tensor(out=sbias[:], in0=b1f[:],
                                           scalar=0.1, in1=phi[:],
                                           op0=ALU.mult, op1=ALU.add))
        v(lambda e: e.memset(carry[:], 0.0), incs=[("CARC", 1)])
        vdrain()

        def ln_stats(nt0, nt, sums=None):
            if sums is None:
                sums = s_sum
            sl = np.s_[:, nt0:nt0 + nt]
            vdrain()
            v(lambda e, sums=sums: e.tensor_single_scalar(
                out=s_m[sl], in_=sums[sl], scalar=1.0 / D, op=ALU.mult))
            v(lambda e, sums=sums: e.tensor_mul(out=s_q[sl], in0=sums[sl],
                                                in1=sums[sl]))
            vdrain()
            v(lambda e: e.scalar_tensor_tensor(out=s_v[sl], in0=s_q[sl],
                                               scalar=-1.0 / D, in1=s_sq[sl],
                                               op0=ALU.mult, op1=ALU.add),
              incs=[("LNA", 1)])
            s_(lambda e: e.activation(out=s_s[sl], in_=s_v[sl],
                                      func=AF.Sqrt, scale=1.0 / D,
                                      bias=epsb[:]),
               incs=[("LNB", 1)], waits=["LNA"])
            sch.wait(V, "LNB")
            v(lambda e: e.reciprocal(out=s_r[sl], in_=s_s[sl]))
            vdrain()
            v(lambda e: e.tensor_mul(out=s_mr[sl], in0=s_m[sl], in1=s_r[sl]))
            vdrain()

        sch.op(S_, None, waits=[("PBC", 4)])
        n_out_box = [0]

        def x_load(gg):
            buf0 = xy[gg % 2]
            for cc in range(GC):
                c = gg * GC + cc
                for j in range(NT_C):
                    sch.op(Y, lambda e, c=c, cc=cc, j=j, buf=buf0:
                           e.dma_start(
                            out=buf[:, cc * NT_C + j, :],
                            in_=x_h[2 * j:2 * j + 2,
                                    c * TS:(c + 1) * TS, :]),
                           incs=[("XLD", 16)])

        def attn_a(g, cc):
            gx = xy[g % 2]
            c = g * GC + cc
            nt0 = cc * NT_C
            for j in range(NT_C):
                xt = gx[:, nt0 + j, :]
                v(lambda e, xt=xt, nj=nt0 + j: e.reduce_sum(
                    out=s_sum[:, nj:nj + 1], in_=xt, axis=AX.X),
                  waits=(["XLD"] if j == 0 else ()))
                s_(lambda e, xt=xt, nj=nt0 + j: e.activation(
                    out=sq[:], in_=xt, func=AF.Square,
                    accum_out=s_sq[:, nj:nj + 1]),
                   incs=[("LN1S", 1)], waits=(["XLD"] if j == 0 else ()))
            sch.wait(V, "LN1S")
            ln_stats(nt0, NT_C)
            for j in range(NT_C):
                xt = gx[:, nt0 + j, :]
                v(lambda e, xt=xt, j=j, nj=nt0 + j: e.tensor_scalar(
                    out=xln[:, j, :], in0=xt,
                    scalar1=s_m[:, nj:nj + 1], scalar2=s_r[:, nj:nj + 1],
                    op0=ALU.subtract, op1=ALU.mult),
                  waits=(["TRX"] if j == 0 else ()))
            vmargin("LN1V")
            sch.wait(T, "LN1V", ("VST", 1), ("W0L", 48), ("BBC", 2))
            for ic in range(DC):
                for j in range(NT_C):
                    t_(lambda e, ic=ic, j=j: e.matmul(
                        ptr[0:128, j * 128:(j + 1) * 128],
                        xln[:, j, ic * 128:(ic + 1) * 128], ident[:],
                        start=True, stop=True),
                       incs=([("TRX", 1)] if j == NT_C - 1 else ()),
                       waits=(["EVT"] if j == 0 else ()))
                s_(lambda e, ic=ic: e.activation(
                    out=xlnf[:, ic, :], in_=ptr[0:128, 0:ROWS],
                    func=AF.Copy), incs=[("EVT", 1)], waits=["TRX"])
            smargin("EVT")
            for ncc in range(DC):
                for ic in range(DC):
                    t_(lambda e, ncc=ncc, ic=ic: e.matmul(
                        pbx[0:128, 0:ROWS],
                        bwt[:, ic, ncc * 128:(ncc + 1) * 128],
                        xlnf[:, ic, :],
                        start=(ic == 0), stop=(ic == DC - 1)),
                       incs=([("BX", 1)] if ic == DC - 1 else ()),
                       waits=((["EVT", "BXE"]) if ic == 0 else ()))
                v(lambda e, ncc=ncc: e.tensor_copy(
                    out=bxb.ap().rearrange(
                        "p (s nc j b1) -> p nc j b1 s", nc=DC, j=2, b1=2)
                    [:, ncc, :, :, :],
                    in_=pbx.ap()[0:128, 0:ROWS].rearrange(
                        "p (j b1 s) -> p j b1 s", j=2, b1=2)),
                  incs=[("BXE", 1)], waits=["BX"])

        def scan_chunk(g, cc, epi=None):
            gx = xy[g % 2]
            nt0 = cc * NT_C
            fcs = list(epi[1]) if epi else []
            sch.wait(G, "BXE")
            for t in range(TS):
                prev = carry[:] if t == 0 else hbuf[:, (t - 1) * BN:t * BN]
                sch.op(G, lambda e, prev=prev: e.tensor_mul(
                    out=tmp32[:], in0=prev, in1=dB[:]),
                  waits=(["SCS"] if t > 0 else ["CARC"]))
                sch.op(G, lambda e, t=t: e.tensor_add(
                    out=tmp32[:], in0=tmp32[:],
                    in1=bxb[:, t * BN:(t + 1) * BN]), incs=[("SCV", 1)])
                s_(lambda e, t=t: e.activation(
                    out=hbuf[:, t * BN:(t + 1) * BN], in_=tmp32[:],
                    func=AF.Tanh), incs=[("SCS", 1)], waits=["SCV"])
                if fcs and t >= 12 and t % 3 == 0:
                    ffn_w1_VS(epi[0], fcs.pop(0))
            while fcs:
                ffn_w1_VS(epi[0], fcs.pop(0))
            sch.wait(V, "SCS")
            v(lambda e: e.tensor_copy(out=carry[:],
                                      in_=hbuf[:, (TS - 1) * BN:]),
              incs=[("CARC", 1)])
            for ncc in range(DC):
                v(lambda e, ncc=ncc: e.tensor_copy(
                    out=hbf[:, ncc * ROWS:(ncc + 1) * ROWS],
                    in_=hbuf.ap().rearrange(
                        "p (s nc j b1) -> p nc j b1 s", nc=DC, j=2, b1=2)
                    [:, ncc, :, :, :]),
                  waits=(["WO"] if ncc == 0 else ()))
            vmargin("HCA")

        def wo_ln2(g, cc):
            gx = xy[g % 2]
            nt0 = cc * NT_C
            sch.wait(T, "HCA")
            for j in range(NT_C):
                for ncc in range(DC):
                    lap = hbf[:, (ncc * 2 + j) * 128:
                              (ncc * 2 + j) * 128 + 128]
                    for half in range(2):
                        t_(lambda e, ncc=ncc, half=half, lap=lap: e.matmul(
                            pwo[0:128, half * 512:(half + 1) * 512],
                            lap,
                            wot[:, ncc, half * 512:(half + 1) * 512],
                            start=(ncc == 0), stop=(ncc == DC - 1)),
                           incs=([("WO", 1)] if (ncc == DC - 1 and
                                                 half == 1) else ()),
                           waits=(["LN2V"] if (ncc == 0 and half == 0)
                                  else ()))
                sch.wait(V, "WO")
                v(lambda e, j=j, nj=nt0 + j: e.scalar_tensor_tensor(
                    out=ao[:, j, :], in0=pwo[0:128, 0:1024], scalar=0.0,
                    in1=bo_bc[:], op0=ALU.bypass, op1=ALU.add,
                    accum_out=s_sum[:, nj:nj + 1]), incs=[("LN2V", 1)])
                s_(lambda e, j=j, nj=nt0 + j: e.activation(
                    out=sq[:], in_=ao[:, j, :], func=AF.Square,
                    accum_out=s_sq[:, nj:nj + 1]),
                   incs=[("LN2S", 1)], waits=["LN2V"])
            sch.wait(V, "LN2S")
            ln_stats(nt0, NT_C)
            for j in range(NT_C):
                xt = gx[:, nt0 + j, :]
                v(lambda e, xt=xt, nj=nt0 + j: e.tensor_scalar(
                    out=xt, in0=xt, scalar1=s_mr[:, nj:nj + 1],
                    scalar2=None, op0=ALU.subtract))
                vdrain()
                v(lambda e, xt=xt, j=j, nj=nt0 + j: e.scalar_tensor_tensor(
                    out=xt, in0=ao[:, j, :],
                    scalar=s_r[:, nj:nj + 1], in1=xt,
                    op0=ALU.mult, op1=ALU.add,
                    accum_out=s_sum3[:, nj:nj + 1]), incs=[("YST", 1)])

        def ln3_try(g):
            gx = xy[g % 2]
            for j in range(NT_G):
                s_(lambda e, j=j, buf=gx: e.activation(
                    out=sq[:], in_=buf[:, j, :], func=AF.Square,
                    accum_out=s_sq[:, j:j + 1]),
                   incs=[("LN3S", 1)], waits=(["YST"] if j == 0 else ()))
            sch.wait(V, "LN3S")
            ln_stats(0, NT_G, sums=s_sum3)
            for j in range(NT_G):
                v(lambda e, j=j, buf=gx: e.tensor_scalar(
                    out=yln[:, j, :], in0=buf[:, j, :],
                    scalar1=s_m[:, j:j + 1], scalar2=s_r[:, j:j + 1],
                    op0=ALU.subtract, op1=ALU.mult),
                  waits=(["TRY"] if j == 0 else ()))
            vmargin("LN3V")
            sch.wait(T, "LN3V")
            for ic in range(DC):
                for j in range(NT_G):
                    t_(lambda e, ic=ic, j=j: e.matmul(
                        ptr[0:128, j * 128:(j + 1) * 128],
                        yln[:, j, ic * 128:(ic + 1) * 128], ident[:],
                        start=True, stop=True),
                       incs=([("TRY", 1)] if j == NT_G - 1 else ()),
                       waits=(["EVT"] if j == 0 else ()))
                s_(lambda e, ic=ic: e.activation(
                    out=ylnf[:, ic, :], in_=ptr[0:128, 0:GROWS],
                    func=AF.Copy), incs=[("EVT", 1)], waits=["TRY"])
            smargin("EVT")

        def ffn_w1_T(g, lo, hi):
            def w1load(pc, waits=()):
                sch.op(Y, lambda e, pc=pc: e.dma_start(
                    out=w1p[pc % 2].ap().rearrange("p k c f -> p (k c f)"),
                    in_=w1scr[pc]),
                    incs=[("W1L", 16)], waits=waits)

            if lo == 0:
                for pc in range(2):
                    w = [("SCW", 16 * 41)]
                    if g > 0:
                        w.append(("W1M", 32 * g - 32 + 4 * (pc + 7)))
                    w1load(pc, waits=w)
                sch.op(T, None, waits=[("EVT", 27 * (g + 1))])
            for fc in range(lo, hi):
                pb = pw1[fc % 2]
                tw = [("W1L", 16 * (9 * g + min(fc // 4 + 2, 9)))]
                if fc >= 2:
                    tw.append(("QV", 32 * g + fc - 1))
                else:
                    tw.append(("FOE", 4 * g))
                for ic in range(DC):
                    t_(lambda e, fc=fc, ic=ic, pb=pb: e.matmul(
                        pb, w1p[(fc // 4) % 2][:, fc % 4, ic, :],
                        ylnf[:, ic, :],
                        start=(ic == 0), stop=(ic == DC - 1)),
                       incs=([("W1M", 1)] if ic == DC - 1 else ()),
                       waits=(tw if ic == 0 else ()))
                if fc % 4 == 3 and fc // 4 + 2 < 8:
                    w1load(fc // 4 + 2, waits=[("W1M", 32 * g + fc + 1)])
                if fc == FC - 1:
                    sch.op(Y, lambda e: e.dma_start(
                        out=dummy_sb[:], in_=w1_h[0:128, 0:64]),
                        incs=[("W1L", 16)])

        def ffn_w1_VS(g, fc):
            pb = pw1[fc % 2]
            s_(lambda e, fc=fc, pb=pb: e.activation(
                out=sinb[:], in_=pb, func=AF.Sin,
                scale=0.1, bias=sbias[:, fc:fc + 1]),
               incs=[("SIN", 1)],
               waits=[("W1M", 32 * g + fc + 1), ("QV", 32 * g + fc)])
            v(lambda e, fc=fc, pb=pb: e.scalar_tensor_tensor(
                out=qbuf[:], in0=sinb[:], scalar=cpr[:, fc:fc + 1],
                in1=pb, op0=ALU.mult, op1=ALU.add),
              incs=[("QV", 1)],
              waits=[("SIN", 32 * g + fc + 1), ("ACT", 33 * g + fc)])
            v(lambda e, fc=fc: e.tensor_scalar(
                out=act[:, fc, :], in0=qbuf[:],
                scalar1=b1f[:, fc:fc + 1], scalar2=0.0,
                op0=ALU.add, op1=ALU.max), incs=[("ACT", 1)],
              waits=[("QV", 32 * g + fc + 1)] +
              ([("W2F", 64 * g)] if fc == 0 else []))
            if fc == FC - 1:
                vmargin("ACT")

        def ffn_w2_T(g, jps):
            def w2load(pc, P, waits=()):
                sch.op(Y, lambda e, pc=pc: e.dma_start(
                    out=w2p[pc % 2].ap().rearrange("p k o -> p (k o)"),
                    in_=w2scr[pc]),
                    incs=[("W2L", 16)], waits=waits)

            for jp in jps:
                P = 2 * g + jp
                for pc in range(2):
                    w = [("SCW", 16 * 41)]
                    if P > 0:
                        w.append(("W2F", 32 * P - 32 + 4 * (pc + 7)))
                    w2load(pc, P, waits=w)
                base_w2l = 16 * 9 * P
                sch.op(T, None, waits=[("ACT", 33 * (g + 1)),
                                       ("QV", 32 * (g + 1)),
                                       ("FOE", 4 * g + 2 * jp)])
                for fc in range(FC):
                    for jj in range(2):
                        j = 2 * jp + jj
                        for half in range(2):
                            last = (jj == 1 and half == 1)
                            t_(lambda e, fc=fc, half=half, j=j, jj=jj:
                               e.matmul(
                                pw2h[jj][half],
                                act[:, fc, j * 128:(j + 1) * 128],
                                w2p[(fc // 4) % 2][:, fc % 4,
                                    half * 512:(half + 1) * 512],
                                start=(fc == 0), stop=(fc == FC - 1)),
                               incs=([("W2F", 1)] if last else ()),
                               waits=([("W2L", base_w2l +
                                         16 * min(fc // 4 + 2, 9))]
                                      if (jj == 0 and half == 0) else ()))
                    if fc % 4 == 3 and fc // 4 + 2 < 8:
                        w2load(fc // 4 + 2, P,
                               waits=[("W2F", 32 * P + fc + 1)])
                    if fc == FC - 1:
                        sch.op(Y, lambda e: e.dma_start(
                            out=dummy_sb[:], in_=w1_h[0:128, 0:64]),
                            incs=[("W2L", 16)])

        def ffn_w2_VS(g, jps):
            gx = xy[g % 2]
            for jp in jps:
                P = 2 * g + jp
                sch.op(V, None, waits=[("W2F", 32 * (P + 1))])
                for jj in range(2):
                    j = 2 * jp + jj
                    v(lambda e, jj=jj, j=j: e.scalar_tensor_tensor(
                        out=fo[:, jj, :], in0=pw2[jj], scalar=0.0,
                        in1=b2_bc[:], op0=ALU.bypass, op1=ALU.add,
                        accum_out=s_sum[:, j:j + 1]), incs=[("FOE", 1)])
                    s_(lambda e, jj=jj, j=j: e.activation(
                        out=sq[:], in_=fo[:, jj, :], func=AF.Square,
                        accum_out=s_sq[:, j:j + 1]),
                       incs=[("LN4S", 1)], waits=["FOE"])
                sch.wait(V, "LN4S")
                ln_stats(2 * jp, 2)
                for jj in range(2):
                    j = 2 * jp + jj
                    xt = gx[:, j, :]
                    v(lambda e, xt=xt, j=j: e.tensor_scalar(
                        out=xt, in0=xt, scalar1=s_mr[:, j:j + 1],
                        scalar2=None, op0=ALU.subtract))
                    vdrain()
                    v(lambda e, xt=xt, j=j, jj=jj: e.scalar_tensor_tensor(
                        out=xt, in0=fo[:, jj, :], scalar=s_r[:, j:j + 1],
                        in1=xt, op0=ALU.mult, op1=ALU.add),
                      incs=[("LN4V", 1)])
            jp = jps[-1]
            cc = jp
            c = g * GC + cc
            sch.wait(Y, "LN4V")
            for j in range(NT_C):
                sch.op(Y, lambda e, c=c, cc=cc, j=j, buf=gx: e.dma_start(
                    out=o_h[2 * j:2 * j + 2, c * TS:(c + 1) * TS, :],
                    in_=buf[:, cc * NT_C + j, :]),
                    incs=[("OUT", 16)])
                n_out_box[0] += 1
            if g + 2 < NG:
                sch.op(Y, None, waits=[("OUT", 16 * n_out_box[0])])
                c2 = (g + 2) * GC + cc
                for j in range(NT_C):
                    sch.op(Y, lambda e, c2=c2, cc=cc, j=j, buf=gx:
                           e.dma_start(
                            out=buf[:, cc * NT_C + j, :],
                            in_=x_h[2 * j:2 * j + 2,
                                    c2 * TS:(c2 + 1) * TS, :]),
                           incs=[("XLD", 16)])

        x_load(0)
        x_load(1)
        attn_a(0, 0)
        for g in range(NG):
            if g > 0:
                ffn_w1_T(g - 1, 0, 16)
            scan_chunk(g, 0,
                       epi=((g - 1, range(0, 16)) if g > 0 else None))
            attn_a(g, 1)
            wo_ln2(g, 0)
            if g > 0:
                ffn_w1_T(g - 1, 16, 32)
                ffn_w2_T(g - 1, [0])
            scan_chunk(g, 1,
                       epi=((g - 1, range(16, 32)) if g > 0 else None))
            if g > 0:
                ffn_w2_VS(g - 1, [0])
            if g + 1 < NG:
                attn_a(g + 1, 0)
            wo_ln2(g, 1)
            if g > 0:
                ffn_w2_T(g - 1, [1])
                ffn_w2_VS(g - 1, [1])
            ln3_try(g)
        glast = NG - 1
        ffn_w1_T(glast, 0, 32)
        for fc in range(FC):
            ffn_w1_VS(glast, fc)
        ffn_w2_T(glast, [0])
        ffn_w2_VS(glast, [0])
        ffn_w2_T(glast, [1])
        ffn_w2_VS(glast, [1])

        sch.op(Y, None, waits=[("OUT", 16 * n_out_box[0])])

        with nc.Block() as block:
            sch.replay(block)

    return nc


_NC_CACHE = None


def kernel(L0_attn_x, L0_attn_A, L0_attn_B_w, L0_attn_Wo, L0_attn_bo,
           L0_ffn_W1, L0_ffn_b1, L0_ffn_quantum_phases,
           L0_ffn_quantum_weights_real, L0_ffn_quantum_weights_imag,
           L0_ffn_W2, L0_ffn_b2):
    global _NC_CACHE
    x = np.ascontiguousarray(np.asarray(L0_attn_x, dtype=np.float32))
    shared = {
        "bwt": np.ascontiguousarray(np.asarray(L0_attn_B_w, np.float32).T),
        "wot": np.ascontiguousarray(np.asarray(L0_attn_Wo, np.float32).T),
        "w1": np.ascontiguousarray(np.asarray(L0_ffn_W1, np.float32)),
        "w2": np.ascontiguousarray(np.asarray(L0_ffn_W2, np.float32)),
        "a_dec": np.ascontiguousarray(
            np.asarray(L0_attn_A, np.float32).reshape(-1, 128).T),
        "bo": np.ascontiguousarray(np.asarray(L0_attn_bo, np.float32)),
        "b1": np.ascontiguousarray(
            np.asarray(L0_ffn_b1, np.float32).reshape(-1, 128).T),
        "theta": np.ascontiguousarray(
            np.asarray(L0_ffn_quantum_phases, np.float32).reshape(-1, 128).T),
        "wre": np.ascontiguousarray(
            np.asarray(L0_ffn_quantum_weights_real,
                       np.float32).reshape(-1, 128).T),
        "wim": np.ascontiguousarray(
            np.asarray(L0_ffn_quantum_weights_imag,
                       np.float32).reshape(-1, 128).T),
        "b2": np.ascontiguousarray(np.asarray(L0_ffn_b2, np.float32)),
    }
    if _NC_CACHE is None:
        _NC_CACHE = build_nc()
    nc = _NC_CACHE
    in_maps = []
    for k in range(NCORES):
        m = dict(shared)
        m["x"] = np.ascontiguousarray(x[k * B:(k + 1) * B])
        in_maps.append(m)
    res = run_bass_kernel_spmd(nc, in_maps, core_ids=list(range(NCORES)))
    return np.concatenate([r["o"] for r in res.results], axis=0)
